# version 03 (final)
# baseline (speedup 1.0000x reference)
import math
import numpy as np
import ml_dtypes
import concourse.bass as bass
import concourse.mybir as mybir
from concourse.bass_utils import run_bass_kernel_spmd

F32 = mybir.dt.float32
BF16 = mybir.dt.bfloat16
AF = mybir.ActivationFunctionType
ALU = mybir.AluOpType

B, T, C = 8, 2048, 1024
KT = C // 128
TT = T // 128
NH, DH = 16, 64
D_INNER, D_STATE = 2048, 64
DIM_INNER = 2730
W1N = 2 * DIM_INNER
EPS = 1e-5
PAD = 4
ZW = T + PAD

STAGES = ["s0", "attn_a", "ffn1", "attn_b", "ffn2", "s5"]
FFN_NL = 6
DBG_LN = [False]
FFN_EPI = True
DBG_XN = [False]
SSM_NL = 5
SSM_B5 = True
FFN_PARTS = "af"
ENGINES = ("pe", "act", "dve", "pool", "sp")


def host_consts():
    c = {}
    c["ident_bf"] = np.eye(128, dtype=ml_dtypes.bfloat16)
    c["ident_f32"] = np.eye(128, dtype=np.float32)
    ii, jj = np.mgrid[0:128, 0:128]
    c["maskU"] = (ii <= jj).astype(ml_dtypes.bfloat16)
    c["maskL"] = (ii >= jj).astype(ml_dtypes.bfloat16)
    c["ones_col_bf"] = np.ones((128, 1), dtype=ml_dtypes.bfloat16)
    half = np.arange(0, DH, 2, dtype=np.float64)
    inv_freq = 1.0 / 10000.0 ** (half / DH)
    t = np.arange(T, dtype=np.float64)
    freqs = t[:, None] * inv_freq[None, :]
    freqs = np.concatenate([freqs, freqs], -1)
    base_scale = (half + 0.4 * DH) / (1.4 * DH)
    power = (t - T // 2) / (256 // 2)
    scale = base_scale[None, :] ** power[:, None]
    scale = np.concatenate([scale, scale], -1)
    cos, sin = np.cos(freqs), np.sin(freqs)
    sign = np.where(np.arange(DH) < 32, -1.0, 1.0)[None, :]

    def ttmaj(x):
        return np.ascontiguousarray(
            x.reshape(TT, 128, DH).transpose(1, 0, 2).reshape(128, TT * DH)
            .astype(np.float32))
    c["rotAq"] = ttmaj(cos * scale)
    c["rotBq"] = ttmaj(sign * sin * scale)
    c["rotAk"] = ttmaj(cos / scale)
    c["rotBk"] = ttmaj(sign * sin / scale)
    tau, tl = np.mgrid[0:128, 0:128]
    c["decayD"] = np.where(tau <= tl, (-0.5) ** (tl - tau), 0.0).astype(np.float32)
    dc = (-0.5) ** (np.arange(128) + 1.0)
    c["decayCol"] = np.broadcast_to(dc, (64, 128)).astype(np.float32).copy()
    c["decayWu"] = ((-0.5) ** (127.0 - np.arange(128)))[:, None].astype(np.float32)
    c["ones_row_f32"] = np.ones((1, 128), dtype=np.float32)
    return c


class Res:
    __slots__ = ("w", "rs")

    def __init__(self, init_ev=None):
        self.w = init_ev
        self.rs = []


class Ev:
    __slots__ = ("eng", "val", "force", "chan")

    def __init__(self, eng, val, force, chan=None):
        self.eng, self.val, self.force, self.chan = eng, val, force, chan


class Chan:
    __slots__ = ("sem", "cnt", "name")

    def __init__(self, nc, name):
        self.sem = nc.alloc_semaphore("ch_" + name)
        self.cnt = 0
        self.name = name


class Prog:
    SERIALIZE = False
    SELF_EDGES = True

    def __init__(self, nc):
        self.nc = nc
        self.sem = {e: nc.alloc_semaphore(f"sem_{e}") for e in ENGINES}
        self.cnt = {e: 0 for e in ENGINES}
        self.known = {e: {} for e in ENGINES}
        self.thunks = {e: [] for e in ENGINES}
        self.chans = {}
        self.n_ops = 0

    def chan(self, name):
        if name not in self.chans:
            self.chans[name] = Chan(self.nc, name)
        return self.chans[name]

    def frontier_res(self):
        r = Res()
        r.w = self.frontier()
        return [r]

    def frontier(self):
        evs = [Ev(e, self.cnt[e], True) for e in ENGINES if self.cnt[e] > 0]
        evs += [Ev(None, c.cnt, True, c) for c in self.chans.values()
                if c.cnt > 0]
        return evs

    def _sched(self, eng, reads, writes):
        if Prog.SERIALIZE:
            waits = []
            kn = self.known[eng]
            for e in ENGINES:
                if e == eng or self.cnt[e] == 0:
                    continue
                if kn.get(("e", e), 0) < self.cnt[e]:
                    waits.append((self.sem[e], self.cnt[e]))
                    kn[("e", e)] = self.cnt[e]
            for c in self.chans.values():
                if c.cnt and kn.get(("c", id(c)), 0) < c.cnt:
                    waits.append((c.sem, c.cnt))
                    kn[("c", id(c))] = c.cnt
            return waits
        evs = []
        for r in reads:
            if r.w is not None:
                evs.extend(r.w if isinstance(r.w, list) else [r.w])
        for w in writes:
            if w.w is not None:
                evs.extend(w.w if isinstance(w.w, list) else [w.w])
            evs.extend(w.rs)
        need = {}
        for ev in evs:
            if ev.chan is not None:
                key = ("c", id(ev.chan))
                v = ev.chan.cnt
                sem = ev.chan.sem
            elif ev.eng == eng:
                if not ev.force and not Prog.SELF_EDGES:
                    continue
                key = ("e", eng)
                v = ev.val
                sem = self.sem[eng]
            else:
                key = ("e", ev.eng)
                v = ev.val
                sem = self.sem[ev.eng]
            cur = need.get(key)
            if cur is None or v > cur[1]:
                need[key] = (sem, v)
        waits = []
        kn = self.known[eng]
        for key, (sem, v) in need.items():
            if kn.get(key, 0) >= v:
                continue
            waits.append((sem, v))
            kn[key] = v
        return waits

    def op(self, eng, emit, reads=(), writes=(), hazard=False):
        waits = self._sched(eng, reads, writes)
        self.cnt[eng] += 1
        ev = Ev(eng, self.cnt[eng], hazard)
        self.thunks[eng].append((waits, emit, self.sem[eng], 1))
        for w in writes:
            w.w = ev
            w.rs = []
        for r in reads:
            r.rs.append(ev)
        self.n_ops += 1
        return ev

    def dma(self, eng, chan, emit, reads=(), writes=()):
        if isinstance(chan, str):
            chan = self.chan(chan)
        waits = self._sched(eng, reads, writes)
        chan.cnt += 16
        ev = Ev(eng, chan.cnt, True, chan)
        self.thunks[eng].append((waits, emit, chan.sem, 16))
        for w in writes:
            w.w = ev
            w.rs = []
        for r in reads:
            r.rs.append(ev)
        self.n_ops += 1
        return ev

    def finalize(self, block):
        chans = list(self.chans.values())

        def mk(eng_name):
            def run(h):
                for waits, emit, sem, units in self.thunks[eng_name]:
                    for s, v in waits:
                        h.wait_ge(s, v)
                    emit(h).then_inc(sem, units)
                if eng_name == "sp":
                    for c in chans:
                        if c.cnt > 0:
                            h.wait_ge(c.sem, c.cnt)
            return run
        block.tensor(mk("pe"))
        block.scalar(mk("act"))
        block.vector(mk("dve"))
        block.gpsimd(mk("pool"))
        block.sync(mk("sp"))


class Buf:
    def __init__(self, aps, ress):
        self.aps, self.r = aps, ress

    def ap(self, i=0):
        return self.aps[i]

    def __len__(self):
        return len(self.aps)


class Region:
    def __init__(self, nc, name, bytes_per_part):
        self.t = nc.alloc_sbuf_tensor(name, [128, bytes_per_part // 4], F32)
        self.cap = bytes_per_part
        self.off = 0
        self.init_ev = None
        self.hi = 0

    def reset(self, prog):
        self.off = 0
        self.init_ev = prog.frontier()

    def carve(self, ntiles, width, dtype):
        sz = 4 if dtype == F32 else 2
        base = self.t[:]
        if dtype == BF16:
            base = base.bitcast(BF16)
        aps, ress = [], []
        step = width * sz
        self.off = (self.off + 3) & ~3
        for i in range(ntiles):
            off_e = (self.off + i * step) // sz
            aps.append(base[:, off_e:off_e + width])
            ress.append(Res(list(self.init_ev) if self.init_ev else None))
        self.off += ntiles * step
        self.hi = max(self.hi, self.off)
        assert self.off <= self.cap, f"region overflow {self.off}>{self.cap}"
        return Buf(aps, ress)

    def claim(self, prog, buf):
        fr = prog.frontier()
        for i in range(len(buf.r)):
            buf.r[i] = Res(list(fr))


class PsumPool:
    def __init__(self, nc):
        self.f32 = [nc.alloc_psum_tensor(f"psf{i}", [128, 512], F32)
                    for i in range(6)]
        self.bf = [nc.alloc_psum_tensor(f"psb{i}", [128, 1024], BF16)
                   for i in range(2)]
        self.f32_r = [Res() for _ in range(6)]
        self.bf_r = [Res() for _ in range(2)]
        self.i_f = 0
        self.i_b = 0

    def mm(self):
        i = self.i_f
        self.i_f = (self.i_f + 1) % 4
        return self.f32[i], self.f32_r[i]

    def stat(self, which):
        return self.f32[4 + which], self.f32_r[4 + which]

    def tr(self):
        i = self.i_b
        self.i_b = (self.i_b + 1) % 2
        return self.bf[i], self.bf_r[i]


def build_program(stages):
    nc = bass.Bass("TRN2", target_bir_lowering=False)
    CN = host_consts()
    dram = {}

    def din(name, shape, dtype=F32):
        dram[name] = nc.dram_tensor(name, list(shape), dtype,
                                    kind="ExternalInput").ap()
        return dram[name]

    x_in = din("x", (T, C))
    for nm, sh in [
        ("aWq", (C, C)), ("aWk", (C, C)), ("aWv", (C, C)), ("aWo", (C, C)),
        ("abq", (1, C)), ("abk", (1, C)), ("abv", (1, C)), ("abo", (1, C)),
        ("bWi", (5, C, 2176)), ("bconvw", (5, D_INNER, 4)),
        ("bconvb", (5, 1, D_INNER)), ("bWo", (5, D_INNER, C)),
        ("bssmg", (5, 1, C)), ("bssmb", (5, 1, C)),
        ("bwq", (C, C)), ("bwk", (C, C)), ("bwv", (C, C)), ("bwo", (C, C)),
        ("batng", (1, C)), ("batnb", (1, C)),
        ("fWqkv", (6, C, 3 * C)), ("fqs", (6, 1, DH)), ("fks", (6, 1, DH)),
        ("fWo", (6, C, C)), ("flng", (6, 1, C)), ("flnb", (6, 1, C)),
        ("fW1", (6, C, W1N)), ("fb1", (6, 1, W1N)),
        ("fW2", (6, DIM_INNER, C)), ("fb2", (6, 1, C)),
    ]:
        din(nm, sh)
    for nm, arr in CN.items():
        dt = BF16 if arr.dtype == ml_dtypes.bfloat16 else F32
        din("c_" + nm, arr.shape, dt)

    out_t = nc.dram_tensor("out", [T, C], F32, kind="ExternalOutput").ap()
    dbg_t = nc.dram_tensor("dbg", [C, T], F32, kind="ExternalOutput").ap()
    xfm = nc.dram_tensor("xfm", [C, T], F32).ap()
    zdram = nc.dram_tensor("zdram", [C, T], F32).ap()
    bcast_dram = nc.dram_tensor("bcd", [2, 1024], F32).ap()

    prog = Prog(nc)
    ps = PsumPool(nc)
    ZD_CELLS = [None]

    r_xfm = [[Res() for _ in range(4)] for _ in range(KT)]
    r_zdram = Res()
    r_bcd = [Res() for _ in range(2)]
    r_out = Res()

    zreg = Region(nc, "zreg", KT * ZW * 4 + 512)
    zbreg = Region(nc, "zbreg", KT * ZW * 2)
    creg = Region(nc, "creg", 4 * 1024)
    oreg = Region(nc, "oreg", 97 * 1024)
    z = zreg.carve(KT, ZW, F32)
    zbf = zbreg.carve(KT, ZW, BF16)

    cbuf = {}
    for nm in ["ident_bf", "maskU", "maskL", "ones_col_bf"]:
        cbuf[nm] = creg.carve(1, CN[nm].shape[1], BF16)
    for nm in ["ident_f32", "decayD", "decayWu", "ones_row_f32"]:
        cbuf[nm] = creg.carve(1, CN[nm].shape[1], F32)
    cbuf["decayCol"] = creg.carve(1, 128, F32)
    epsb = creg.carve(1, 8, F32)
    ONES_BF = cbuf["ones_col_bf"].ap(0)
    ONES_BF_R = cbuf["ones_col_bf"].r[0]
    IDB, IDB_R = cbuf["ident_bf"].ap(0), cbuf["ident_bf"].r[0]
    IDF, IDF_R = cbuf["ident_f32"].ap(0), cbuf["ident_f32"].r[0]

    xfm_pkt = xfm.rearrange("(k p) t -> p k t", p=128)
    z_pkt = zdram.rearrange("(k p) t -> p k t", p=128)

    def xfm_cells(c0, w):
        g0, g1 = c0 // 512, (c0 + w - 1) // 512
        return [r_xfm[k][g] for k in range(KT) for g in range(g0, g1 + 1)]

    def load_consts():
        for nm in ["ident_bf", "maskU", "maskL", "ones_col_bf",
                   "ident_f32", "decayD", "decayWu"]:
            prog.dma("sp", "cc", lambda h, nm=nm: h.dma_start(
                cbuf[nm].ap(0), dram["c_" + nm]),
                writes=[cbuf[nm].r[0]])
        prog.dma("sp", "cc", lambda h: h.dma_start(
            cbuf["decayCol"].ap(0)[0:64, :], dram["c_decayCol"]),
            writes=[cbuf["decayCol"].r[0]])
        prog.dma("sp", "cc", lambda h: h.dma_start(
            cbuf["decayCol"].ap(0)[64:128, :], dram["c_decayCol"]),
            writes=[cbuf["decayCol"].r[0]])
        prog.dma("sp", "cc", lambda h: h.dma_start(
            cbuf["ones_row_f32"].ap(0)[0:1, :], dram["c_ones_row_f32"]),
            writes=[cbuf["ones_row_f32"].r[0]])
        prog.op("dve", lambda h: h.memset(epsb.ap(0)[0:1, 0:1], float(EPS)),
                writes=[epsb.r[0]])
        for k in range(KT):
            prog.op("dve", lambda h, k=k: h.memset(z.ap(k)[:, 0:PAD], 0.0),
                    writes=[z.r[k]])
            prog.op("dve", lambda h, k=k: h.memset(zbf.ap(k)[:, 0:PAD], 0.0),
                    writes=[zbf.r[k]])

    def load_fm(buf, chan, dsrc_pkt, c0, w, cells, q="sp"):
        prog.dma(q, chan, lambda h: h.dma_start(
            _grp_ap(buf, w), dsrc_pkt[:, :, c0:c0 + w]),
            reads=cells, writes=list(buf.r))

    def store_fm(buf, dsrc_pkt, c0, w, cells, chan="xfm"):
        prog.dma("sp", chan, lambda h: h.dma_start(
            dsrc_pkt[:, :, c0:c0 + w], _grp_ap(buf, w)),
            reads=list(buf.r), writes=cells)

    def _grp_ap(buf, w):
        a0 = buf.ap(0)
        step = (buf.aps[1].offset - a0.offset) if len(buf.aps) > 1 else w
        return bass.AP(a0.tensor, a0.offset,
                       [[a0.ap[0][0], 128], [step, len(buf.aps)], [1, w]])

    colsc_stage = [None]

    def load_cols(dst_ap, dst_res, row_tensor, row_off, ncol, width=128):
        src = bass.AP(row_tensor, row_off, [[width, ncol], [1, width]])
        stg = colsc_stage[0]
        prog.dma("sp", "colsc", lambda h: h.dma_start(
            stg.ap(0)[0:ncol, 0:width], src), writes=[stg.r[0]])
        pb, pr = ps.stat(0)
        prog.op("pe", lambda h: h.transpose(pb[0:width, 0:ncol],
                                            stg.ap(0)[0:ncol, 0:width],
                                            IDF[0:ncol, 0:ncol]),
                reads=[stg.r[0], IDF_R], writes=[pr])
        prog.op("dve", lambda h: h.tensor_copy(dst_ap[0:width, 0:ncol],
                                               pb[0:width, 0:ncol]),
                reads=[pr], writes=[dst_res])

    def s0():
        oreg.reset(prog)
        xin = oreg.carve(4, 1024, F32)
        stg = oreg.carve(KT, 512, F32)
        for grp in range(4):
            src = bass.AP(x_in.tensor, grp * 512 * C,
                          [[C, 128], [128 * C, 4], [1, C]])
            prog.dma("sp", "s0x", lambda h, src=src: h.dma_start(
                _grp_ap4(xin), src), writes=list(xin.r))
            for j in range(4):
                for k in range(KT):
                    pb, pr = ps.mm()
                    prog.op("pe", lambda h, j=j, k=k, pb=pb: h.transpose(
                        pb[:, 0:128], xin.ap(j)[:, 128 * k:128 * (k + 1)], IDF),
                        reads=[xin.r[j], IDF_R], writes=[pr])
                    prog.op("dve", lambda h, j=j, k=k, pb=pb: h.tensor_copy(
                        stg.ap(k)[:, 128 * j:128 * (j + 1)], pb[:, 0:128]),
                        reads=[pr], writes=[stg.r[k]])
            store_fm(stg, xfm_pkt, 512 * grp, 512,
                     [r_xfm[k][grp] for k in range(KT)])

    def _grp_ap4(buf):
        a0 = buf.ap(0)
        return bass.AP(a0.tensor, a0.offset,
                       [[a0.ap[0][0], 128], [1024, 4], [1, 1024]])

    def s5():
        oreg.reset(prog)
        xin = oreg.carve(KT, 512, F32)
        stg = oreg.carve(4, 1024, F32)
        for grp in range(4):
            load_fm(xin, "s5x", xfm_pkt, 512 * grp, 512,
                    [r_xfm[k][grp] for k in range(KT)])
            for j in range(4):
                for k in range(KT):
                    pb, pr = ps.mm()
                    prog.op("pe", lambda h, j=j, k=k, pb=pb: h.transpose(
                        pb[:, 0:128], xin.ap(k)[:, 128 * j:128 * (j + 1)], IDF),
                        reads=[xin.r[k], IDF_R], writes=[pr])
                    prog.op("dve", lambda h, j=j, k=k, pb=pb: h.tensor_copy(
                        stg.ap(j)[:, 128 * k:128 * (k + 1)], pb[:, 0:128]),
                        reads=[pr], writes=[stg.r[j]])
            dst = bass.AP(out_t.tensor, grp * 512 * C,
                          [[C, 128], [128 * C, 4], [1, C]])
            prog.dma("sp", "outst", lambda h, dst=dst: h.dma_start(
                dst, _grp_ap4(stg)), reads=list(stg.r), writes=[r_out])

    bc_slot = [0]

    def ln_chunk(w, fsrc, bfsrc, outw, rows, sq, bcast):
        psx, rx = ps.stat(0)
        psq, rqr = ps.stat(1)
        bfl = [bfsrc(k) for k in range(KT)]
        fl = [fsrc(k) for k in range(KT)]

        def emit_sum(h):
            ins = None
            for k in range(KT):
                ins = h.matmul(psx[0:1, 0:w], ONES_BF, bfl[k][0],
                               start=(k == 0), stop=(k == KT - 1))
            return ins
        prog.op("pe", emit_sum, reads=[ONES_BF_R] + [b[1] for b in bfl],
                writes=[rx])
        for k in range(KT):
            sqa, sqr = sq.ap(k % 2), sq.r[k % 2]
            prog.op("act", lambda h, k=k, sqa=sqa: h.activation(
                sqa[:, 0:w], fl[k][0], AF.Square),
                reads=[fl[k][1]], writes=[sqr])
            prog.op("pe", lambda h, k=k, sqa=sqa: h.matmul(
                psq[0:1, 0:w], ONES_BF, sqa[:, 0:w],
                start=(k == 0), stop=(k == KT - 1)),
                reads=[ONES_BF_R, sqr], writes=[rqr])
        rowA, rA = rows.ap(0), rows.r[0]
        rowB, rB = rows.ap(1), rows.r[1]
        mu = rowA[0:1, 0:512]
        rstd = rowA[0:1, 512:1024]
        var = rowB[0:1, 0:512]
        prog.op("dve", lambda h: h.tensor_scalar_mul(mu[:, 0:w], psx[0:1, 0:w],
                                                     1.0 / C),
                reads=[rx], writes=[rA])
        prog.op("dve", lambda h: h.tensor_mul(var[:, 0:w], mu[:, 0:w],
                                              mu[:, 0:w]),
                reads=[rA], writes=[rB])
        prog.op("dve", lambda h: h.scalar_tensor_tensor(
            var[:, 0:w], psq[0:1, 0:w], 1.0 / C, var[:, 0:w],
            ALU.mult, ALU.subtract),
            reads=[rqr, rB], writes=[rB])
        prog.op("act", lambda h: h.activation(rstd[:, 0:w], var[:, 0:w],
                                              AF.Sqrt,
                                              bias=epsb.ap(0)[0:1, 0:1]),
                reads=[rB, epsb.r[0]], writes=[rA])
        prog.op("dve", lambda h: h.reciprocal(rstd[:, 0:w], rstd[:, 0:w]),
                reads=[rA], writes=[rA], hazard=True)
        ones_f = cbuf["ones_row_f32"]
        pbM, prM = ps.mm()
        prog.op("pe", lambda h: h.matmul(pbM[:, 0:w], ones_f.ap(0)[0:1, :],
                                         mu[:, 0:w], start=True, stop=True),
                reads=[rA, ones_f.r[0]], writes=[prM])
        pbR, prR = ps.mm()
        prog.op("pe", lambda h: h.matmul(pbR[:, 0:w], ones_f.ap(0)[0:1, :],
                                         rstd[:, 0:w], start=True, stop=True),
                reads=[rA, ones_f.r[0]], writes=[prR])
        prog.op("dve", lambda h: h.tensor_copy(bcast.ap(0)[:, 0:w],
                                               pbM[:, 0:w]),
                reads=[prM], writes=[bcast.r[0]])
        prog.op("dve", lambda h: h.tensor_copy(bcast.ap(0)[:, 512:512 + w],
                                               pbR[:, 0:w]),
                reads=[prR], writes=[bcast.r[0]])
        mu_bc = bcast.ap(0)[:, 0:w]
        rstd_bc = bcast.ap(0)[:, 512:512 + w]
        for k in range(KT):
            outw(k, mu_bc, rstd_bc, (bcast.r[0], bcast.r[0]))

    DBG_PT = [("dbgpt" in stages)]
    def dense_attn(kind):
        zreg.reset(prog)
        oreg.reset(prog)
        kTbuf = zreg.carve(KT, 2048, BF16)
        v_ext = zreg.carve(TT, NH * 65, BF16)
        qT = oreg.carve(KT, 512, BF16)
        outk = oreg.carve(2, 1024, BF16)
        aoT = oreg.carve(KT, 512, BF16)
        PT = oreg.carve(3, 2048, BF16)
        wbufA = oreg.carve(1, 1024, BF16)
        wbufB = oreg.carve(1, 1024, BF16)
        wvb = oreg.carve(1, KT * 512, BF16)
        rows = Buf(oreg.carve(1, 1024, F32).aps + oreg.carve(1, 512, F32).aps,
                   [Res(), Res()])
        sqb = oreg.carve(2, 512, BF16)
        bcb = oreg.carve(1, 1024, F32)
        xstg = oreg.carve(KT, 512, F32)
        xbstg = oreg.carve(KT, 512, BF16)
        x2stg = oreg.carve(KT, 512, F32) if kind == "b5" else None
        bias_bc = oreg.carve(1, 1024, F32) if kind == "a" else None
        den = oreg.carve(1, 64, F32)
        bkcol = oreg.carve(1, 32, F32)
        colsc_stage[0] = oreg.carve(1, 128, F32)
        wbufs = [wbufA, wbufB]
        wch = ["wA", "wB"]

        if kind == "a":
            for c in range(4):
                c0 = 512 * c
                cells = [r_xfm[k][c] for k in range(KT)]
                load_fm(xstg, "xst", xfm_pkt, c0, 512, cells)
                load_fm(xbstg, "xbst", xfm_pkt, c0, 512, cells, q="pool")

                def outw(k, mu_bc, rstd_bc, bc_r, c0=c0):
                    prog.op("dve", lambda h, k=k: h.scalar_tensor_tensor(
                        xstg.ap(k), xstg.ap(k), 1.0, mu_bc,
                        ALU.mult, ALU.subtract),
                        reads=[xstg.r[k], bc_r[0], bc_r[1]], writes=[xstg.r[k]])
                    prog.op("dve", lambda h, k=k, c0=c0: h.scalar_tensor_tensor(
                        zbf.ap(k)[:, PAD + c0:PAD + c0 + 512], xstg.ap(k), 1.0,
                        rstd_bc, ALU.mult, ALU.mult),
                        reads=[xstg.r[k], bc_r[0], bc_r[1]], writes=[zbf.r[k]])
                ln_chunk(512, lambda k: (xstg.ap(k), xstg.r[k]),
                         lambda k: (xbstg.ap(k), xbstg.r[k]),
                         outw, rows, sqb, bcb)
            wq, wk, wv, wo = dram["aWq"], dram["aWk"], dram["aWv"], dram["aWo"]
            has_bias = True
        else:
            wq, wk, wv, wo = dram["bwq"], dram["bwk"], dram["bwv"], dram["bwo"]
            has_bias = False

        shsl = [zbf.ap(k)[:, PAD:PAD + T] for k in range(KT)]
        shr = [zbf.r[k] for k in range(KT)]

        if has_bias:
            load_cols(bkcol.ap(0)[:, 0:8], bkcol.r[0], dram["abq"].tensor, 0, 8)
            load_cols(bkcol.ap(0)[:, 8:16], bkcol.r[0], dram["abk"].tensor, 0, 8)
            load_cols(bkcol.ap(0)[:, 16:24], bkcol.r[0], dram["abo"].tensor,
                      0, 8)
            prog.op("dve", lambda h: h.tensor_scalar_mul(
                bkcol.ap(0)[:, 0:8], bkcol.ap(0)[:, 0:8], 0.125),
                reads=[bkcol.r[0]], writes=[bkcol.r[0]])
            prog.dma("sp", "bbc", lambda h: h.dma_start(
                bias_bc.ap(0), bass.AP(dram["abv"].tensor, 0,
                                       [[0, 128], [1, 1024]])),
                writes=[bias_bc.r[0]])
        else:
            load_cols(bkcol.ap(0)[:, 0:8], bkcol.r[0], dram["batng"].tensor,
                      0, 8)
            load_cols(bkcol.ap(0)[:, 8:16], bkcol.r[0], dram["batnb"].tensor,
                      0, 8)

        def w_lhsT(wdram, nt, bi):
            wsrc = wdram.rearrange("(k p) n -> p k n", p=128)
            return prog.dma("pool", wch[bi], lambda h: h.dma_start(
                wbufs[bi].ap(0).rearrange("p (k n) -> p k n", n=128),
                wsrc[:, :, 128 * nt:128 * (nt + 1)]),
                writes=[wbufs[bi].r[0]])

        for nt in range(KT):
            bi = nt % 2
            w_lhsT(wk, nt, bi)
            for tc in range(4):
                pb, pr = ps.mm()

                def emit(h, nt=nt, tc=tc, pb=pb, bi=bi):
                    ins = None
                    w3 = wbufs[bi].ap(0).rearrange("p (k n) -> p k n", n=128)
                    for k in range(KT):
                        ins = h.matmul(pb[:, 0:512], w3[:, k, :],
                                       shsl[k][:, 512 * tc:512 * (tc + 1)],
                                       start=(k == 0), stop=(k == KT - 1))
                    return ins
                prog.op("pe", emit, reads=[wbufs[bi].r[0]] + shr, writes=[pr])
                if has_bias:
                    prog.op("act", lambda h, nt=nt, tc=tc, pb=pb: h.activation(
                        kTbuf.ap(nt)[:, 512 * tc:512 * (tc + 1)], pb[:, 0:512],
                        AF.Identity, bias=bkcol.ap(0)[:, 8 + nt:9 + nt]),
                        reads=[pr, bkcol.r[0]], writes=[kTbuf.r[nt]])
                else:
                    prog.op("act", lambda h, nt=nt, tc=tc, pb=pb: h.activation(
                        kTbuf.ap(nt)[:, 512 * tc:512 * (tc + 1)], pb[:, 0:512],
                        AF.Copy),
                        reads=[pr], writes=[kTbuf.r[nt]])

        for tt in range(TT):
            prog.op("dve", lambda h, tt=tt: h.memset(
                v_ext.ap(tt).rearrange("p (s n) -> p s n", n=65)[:, :, 64:65],
                1.0),
                writes=[v_ext.r[tt]])
        wsrcv = wv.rearrange("(k p) n -> p k n", p=128)
        for nc_ in range(2):
            prog.dma("pool", "wv", lambda h, nc_=nc_: h.dma_start(
                wvb.ap(0).rearrange("p (k n) -> p k n", n=512),
                wsrcv[:, :, 512 * nc_:512 * (nc_ + 1)]),
                writes=[wvb.r[0]])
            for tt in range(TT):
                pb, pr = ps.mm()

                def emit(h, tt=tt, pb=pb):
                    ins = None
                    w3 = wvb.ap(0).rearrange("p (k n) -> p k n", n=512)
                    for k in range(KT):
                        ins = h.matmul(pb[:, 0:512],
                                       shsl[k][:, 128 * tt:128 * (tt + 1)],
                                       w3[:, k, :],
                                       start=(k == 0), stop=(k == KT - 1))
                    return ins
                prog.op("pe", emit, reads=[wvb.r[0]] + shr, writes=[pr])
                v3 = v_ext.ap(tt).rearrange("p (s n) -> p s n", n=65)
                vsl = v3[:, 8 * nc_:8 * (nc_ + 1), 0:64]
                if has_bias:
                    prog.op("dve", lambda h, pb=pb, vsl=vsl, nc_=nc_:
                            h.tensor_add(
                                vsl,
                                pb[:, 0:512].rearrange("p (s n) -> p s n",
                                                       n=64),
                                bias_bc.ap(0)[:, 512 * nc_:512 * (nc_ + 1)]
                                .rearrange("p (s n) -> p s n", n=64)),
                            reads=[pr, bias_bc.r[0]], writes=[v_ext.r[tt]])
                else:
                    prog.op("act", lambda h, pb=pb, vsl=vsl: h.activation(
                        vsl, pb[:, 0:512].rearrange("p (s n) -> p s n", n=64),
                        AF.Copy),
                        reads=[pr], writes=[v_ext.r[tt]])

        maskU, maskU_r = cbuf["maskU"].ap(0), cbuf["maskU"].r[0]
        for it in range(TT):
            cchunk = it // 4
            if it % 4 == 0:
                for nt in range(KT):
                    bi = nt % 2
                    w_lhsT(wq, nt, bi)
                    pb, pr = ps.mm()

                    def emit(h, nt=nt, pb=pb, cchunk=cchunk, bi=bi):
                        ins = None
                        w3 = wbufs[bi].ap(0).rearrange("p (k n) -> p k n",
                                                       n=128)
                        for k in range(KT):
                            ins = h.matmul(
                                pb[:, 0:512], w3[:, k, :],
                                shsl[k][:, 512 * cchunk:512 * (cchunk + 1)],
                                start=(k == 0), stop=(k == KT - 1))
                        return ins
                    prog.op("pe", emit, reads=[wbufs[bi].r[0]] + shr,
                            writes=[pr])
                    if has_bias:
                        prog.op("act", lambda h, nt=nt, pb=pb: h.activation(
                            qT.ap(nt), pb[:, 0:512], AF.Identity,
                            scale=0.125, bias=bkcol.ap(0)[:, nt:nt + 1]),
                            reads=[pr, bkcol.r[0]], writes=[qT.r[nt]])
                    else:
                        prog.op("act", lambda h, nt=nt, pb=pb: h.activation(
                            qT.ap(nt), pb[:, 0:512], AF.Identity, scale=0.125),
                            reads=[pr], writes=[qT.r[nt]])
            ic = 128 * (it % 4)
            ob, obr = outk.ap(it % 2), outk.r[it % 2]
            for g in range(8):
                for sub, hh in ((0, 2 * g), (64, 2 * g + 1)):
                    pi = (2 * g + (sub // 64)) % 3
                    ptile, ptr = PT.ap(pi), PT.r[pi]
                    njt = it + 1
                    for j0 in range(0, njt, 4):
                        jn = min(4, njt - j0)
                        pbS, prS = ps.mm()

                        def emitS(h, j0=j0, jn=jn, hh=hh, sub=sub, pbS=pbS,
                                  ic=ic):
                            ins = None
                            for jj in range(jn):
                                jt = j0 + jj
                                ins = h.matmul(
                                    pbS[:, 128 * jj:128 * (jj + 1)],
                                    kTbuf.ap(hh // 2)[sub:sub + 64,
                                                      128 * jt:128 * (jt + 1)],
                                    qT.ap(hh // 2)[sub:sub + 64, ic:ic + 128],
                                    start=True, stop=True,
                                    tile_position=(sub, 0))
                            return ins
                        prog.op("pe", emitS,
                                reads=[kTbuf.r[hh // 2], qT.r[hh // 2]],
                                writes=[prS])
                        prog.op("act", lambda h, j0=j0, jn=jn, pbS=pbS,
                                ptile=ptile: h.activation(
                                    ptile[:, 128 * j0:128 * (j0 + jn)],
                                    pbS[:, 0:128 * jn], AF.Exp),
                                reads=[prS], writes=[ptr])
                    prog.op("dve", lambda h, ptile=ptile, it=it: h.tensor_mul(
                        ptile[:, 128 * it:128 * (it + 1)],
                        ptile[:, 128 * it:128 * (it + 1)], maskU),
                        reads=[ptr, maskU_r], writes=[ptr])
                    pbO, prO = ps.mm()

                    def emitPV(h, njt=njt, hh=hh, ptile=ptile, pbO=pbO):
                        ins = None
                        for jt in range(njt):
                            ins = h.matmul(
                                pbO[:, 0:65],
                                ptile[:, 128 * jt:128 * (jt + 1)],
                                v_ext.ap(jt)[:, 65 * hh:65 * hh + 65],
                                start=(jt == 0), stop=(jt == njt - 1))
                        return ins
                    prog.op("pe", emitPV,
                            reads=[ptr] + [v_ext.r[jt] for jt in range(njt)],
                            writes=[prO])
                    if it == 15 and hh == 0 and DBG_PT[0]:
                        for q4 in range(4):
                            prog.op("dve", lambda h, q4=q4, ptile=ptile:
                                    h.tensor_copy(
                                        xstg.ap(q4),
                                        ptile[:, 512 * q4:512 * (q4 + 1)]),
                                    reads=[ptr], writes=[xstg.r[q4]])
                        prog.op("dve", lambda h, pbO=pbO: h.tensor_copy(
                            xstg.ap(4)[:, 0:65], pbO[:, 0:65]),
                            reads=[prO], writes=[xstg.r[4]])
                        dbg_pkt9 = dbg_t.rearrange("(k p) t -> p k t", p=128)
                        prog.dma("sp", "dbgs", lambda h: h.dma_start(
                            dbg_pkt9[:, :, 0:512], _grp_ap(xstg, 512)),
                            reads=list(xstg.r), writes=[Res()])
                    prog.op("dve", lambda h, pbO=pbO: h.reciprocal(
                        den.ap(0)[:, 0:1], pbO[:, 64:65]),
                        reads=[prO], writes=[den.r[0]], hazard=True)
                    prog.op("dve", lambda h, pbO=pbO, hh=hh, ob=ob:
                            h.tensor_mul(
                                ob[:, 64 * hh:64 * (hh + 1)].unsqueeze(1),
                                pbO[:, 0:64].unsqueeze(1),
                                den.ap(0)[:, 0:1].unsqueeze(2).broadcast_to(
                                    [128, 1, 64])),
                            reads=[prO, den.r[0]], writes=[obr])
            for dt_ in range(KT):
                pb, pr = ps.tr()
                prog.op("pe", lambda h, dt_=dt_, ob=ob, pb=pb: h.transpose(
                    pb[:, 0:128], ob[:, 128 * dt_:128 * (dt_ + 1)], IDB),
                    reads=[obr, IDB_R], writes=[pr])
                prog.op("dve", lambda h, dt_=dt_, pb=pb, it=it: h.tensor_copy(
                    aoT.ap(dt_)[:, 128 * (it % 4):128 * (it % 4) + 128],
                    pb[:, 0:128]),
                    reads=[pr], writes=[aoT.r[dt_]])
            if it % 4 == 3:
                c0 = 512 * cchunk
                cells = [r_xfm[k][cchunk] for k in range(KT)]
                if kind == "a":
                    load_fm(xstg, "xst", xfm_pkt, c0, 512, cells)
                else:
                    load_fm(xstg, "z2ld", z_pkt, c0, 512, ZD_CELLS[0](c0, 512))
                    load_fm(x2stg, "x2st", xfm_pkt, c0, 512, cells)
                for nt in range(KT):
                    bi = nt % 2
                    w_lhsT(wo, nt, bi)
                    pb, pr = ps.mm()

                    def emit(h, nt=nt, pb=pb, bi=bi):
                        ins = None
                        w3 = wbufs[bi].ap(0).rearrange("p (k n) -> p k n",
                                                       n=128)
                        for k in range(KT):
                            ins = h.matmul(pb[:, 0:512], w3[:, k, :],
                                           aoT.ap(k)[:, 0:512],
                                           start=(k == 0), stop=(k == KT - 1))
                        return ins
                    prog.op("pe", emit,
                            reads=[wbufs[bi].r[0]] + [aoT.r[k]
                                                      for k in range(KT)],
                            writes=[pr])
                    if kind == "a":
                        prog.op("dve", lambda h, nt=nt, pb=pb:
                                h.scalar_tensor_tensor(
                                    xstg.ap(nt), pb[:, 0:512],
                                    bkcol.ap(0)[:, 16 + nt:17 + nt],
                                    xstg.ap(nt), ALU.add, ALU.add),
                                reads=[pr, bkcol.r[0], xstg.r[nt]],
                                writes=[xstg.r[nt]])
                    else:
                        prog.op("dve", lambda h, nt=nt, pb=pb:
                                h.scalar_tensor_tensor(
                                    xstg.ap(nt), pb[:, 0:512], 1.0,
                                    xstg.ap(nt), ALU.mult, ALU.add),
                                reads=[pr, xstg.r[nt]], writes=[xstg.r[nt]])
                if kind == "a":
                    store_fm(xstg, xfm_pkt, c0, 512, cells)
                else:
                    for k in range(KT):
                        prog.op("act", lambda h, k=k: h.activation(
                            xbstg.ap(k), xstg.ap(k), AF.Copy),
                            reads=[xstg.r[k]], writes=[xbstg.r[k]])

                    def outw(k, mu_bc, rstd_bc, bc_r, c0=c0):
                        prog.op("dve", lambda h, k=k: h.scalar_tensor_tensor(
                            xstg.ap(k), xstg.ap(k), 1.0, mu_bc,
                            ALU.mult, ALU.subtract),
                            reads=[xstg.r[k], bc_r[0], bc_r[1]], writes=[xstg.r[k]])
                        prog.op("dve", lambda h, k=k: h.scalar_tensor_tensor(
                            xstg.ap(k), xstg.ap(k), bkcol.ap(0)[:, k:k + 1],
                            rstd_bc, ALU.mult, ALU.mult),
                            reads=[xstg.r[k], bc_r[0], bc_r[1], bkcol.r[0]],
                            writes=[xstg.r[k]])
                        prog.op("act", lambda h, k=k: h.activation(
                            xstg.ap(k), xstg.ap(k), AF.Identity,
                            bias=bkcol.ap(0)[:, 8 + k:9 + k]),
                            reads=[xstg.r[k], bkcol.r[0]], writes=[xstg.r[k]],
                            hazard=True)
                        prog.op("dve", lambda h, k=k: h.tensor_add(
                            xstg.ap(k), xstg.ap(k), x2stg.ap(k)),
                            reads=[xstg.r[k], x2stg.r[k]], writes=[xstg.r[k]])
                    ln_chunk(512, lambda k: (xstg.ap(k), xstg.r[k]),
                             lambda k: (xbstg.ap(k), xbstg.r[k]),
                             outw, rows, sqb, bcb)
                    store_fm(xstg, xfm_pkt, c0, 512, cells)
        return {"kT": kTbuf, "qT": qT, "v_ext": v_ext, "aoT": aoT,
                "outk": outk, "xstg": xstg}

    zbf_grp = bass.AP(zbf.ap(0).tensor, zbf.ap(0).offset + PAD,
                      [[zbf.ap(0).ap[0][0], 128], [ZW, KT], [1, T]])
    r_zd = [[Res() for _ in range(4)] for _ in range(KT)]

    def zd_cells(c0, w):
        g0, g1 = c0 // 512, (c0 + w - 1) // 512
        return [r_zd[k][g] for k in range(KT) for g in range(g0, g1 + 1)]

    def ffnblk():
        maskU, maskU_r = cbuf["maskU"].ap(0), cbuf["maskU"].r[0]
        maskL, maskL_r = cbuf["maskL"].ap(0), cbuf["maskL"].r[0]
        shz = [zbf.ap(k)[:, (PAD if k < 4 else PAD - 1):] for k in range(KT)]
        B = {}

        def ln_carves():
            B["rows"] = oreg.carve(2, 1024, F32)
            B["sqb"] = oreg.carve(2, 512, BF16)
            B["bcb"] = oreg.carve(1, 1024, F32)
            B["zstg2"] = oreg.carve(KT, 512, F32)
            colsc_stage[0] = oreg.carve(1, 128, F32)

        def zbf_refresh():
            prog.dma("pool", "zbfr", lambda h: h.dma_start(zbf_grp, z_pkt),
                     reads=zd_cells(0, T), writes=list(zbf.r))

        def attn_phase(layer):
            zreg.reset(prog)
            oreg.reset(prog)
            kTf = zreg.carve(KT, 2048, BF16)
            v_ext = zreg.carve(TT, NH * 65, BF16)
            rot = oreg.carve(4, 1024, F32)
            rep = oreg.carve(1, 256, F32)
            qT = oreg.carve(KT, 512, BF16)
            qkstg = oreg.carve(2, 512, BF16)
            outk = oreg.carve(2, 1024, BF16)
            aoT = oreg.carve(KT, 512, BF16)
            PT = oreg.carve(3, 384, BF16)
            qsb = oreg.carve(2, 512, F32)
            sqs = oreg.carve(2, 512, F32)
            rt1 = oreg.carve(1, 512, F32)
            rt2 = oreg.carve(1, 512, F32)
            l2t = oreg.carve(1, 64, F32)
            wq_ring = [oreg.carve(1, KT * 512, BF16) for _ in range(2)]
            wo_ring = [oreg.carve(1, KT * 128, BF16) for _ in range(2)]
            zstg2 = oreg.carve(KT, 512, F32)
            den = oreg.carve(1, 64, F32)

            def rotary_unit(pb, pr, gt, qk_scale, A_ap, A_r, B_ap, B_r,
                            slot):
                qa, qr = qsb.ap(slot), qsb.r[slot]
                sa, sr = sqs.ap(slot), sqs.r[slot]
                prog.op("act", lambda h: h.activation(qa, pb[:, 0:512],
                                                      AF.Copy),
                        reads=[pr], writes=[qr])
                prog.op("act", lambda h: h.activation(sa, pb[:, 0:512],
                                                      AF.Square),
                        reads=[pr], writes=[sr])
                prog.op("dve", lambda h: h.reduce_sum(
                    l2t.ap(0)[:, 0:8].unsqueeze(2),
                    sa.rearrange("p (s n) -> p s n", n=64),
                    axis=mybir.AxisListType.X),
                    reads=[sr], writes=[l2t.r[0]], hazard=True)
                prog.op("act", lambda h: h.activation(
                    l2t.ap(0)[:, 8:16], l2t.ap(0)[:, 0:8], AF.Sqrt),
                    reads=[l2t.r[0]], writes=[l2t.r[0]], hazard=True)
                prog.op("dve", lambda h: h.tensor_scalar_max(
                    l2t.ap(0)[:, 8:16], l2t.ap(0)[:, 8:16], 1e-12),
                    reads=[l2t.r[0]], writes=[l2t.r[0]])
                prog.op("dve", lambda h: h.reciprocal(
                    l2t.ap(0)[:, 8:16], l2t.ap(0)[:, 8:16]),
                    reads=[l2t.r[0]], writes=[l2t.r[0]], hazard=True)
                Av = A_ap[:, 64 * gt:64 * (gt + 1)].unsqueeze(1)\
                    .broadcast_to([128, 8, 64])
                Bv = B_ap[:, 64 * gt:64 * (gt + 1)].unsqueeze(1)\
                    .broadcast_to([128, 8, 64])
                q3 = qa.rearrange("p (s n) -> p s n", n=64)
                sig = bass.AP(qa.tensor, qa.offset + 32,
                              [[qa.ap[0][0], 128], [64, 8], [-32, 2],
                               [1, 32]])
                prog.op("dve", lambda h: h.tensor_mul(
                    rt1.ap(0).rearrange("p (s n) -> p s n", n=64), q3, Av),
                    reads=[qr, A_r], writes=[rt1.r[0]])
                prog.op("dve", lambda h: h.tensor_mul(
                    rt2.ap(0).rearrange("p (s a n) -> p s a n", a=2, n=32),
                    sig, Bv.rearrange("p s (a n) -> p s a n", a=2)),
                    reads=[qr, B_r], writes=[rt2.r[0]])
                prog.op("dve", lambda h: h.tensor_add(rt1.ap(0), rt1.ap(0),
                                                      rt2.ap(0)),
                        reads=[rt1.r[0], rt2.r[0]], writes=[rt1.r[0]])
                rbc = l2t.ap(0)[:, 8:16].unsqueeze(2)\
                    .broadcast_to([128, 8, 64])
                prog.op("dve", lambda h: h.scalar_tensor_tensor(
                    qkstg.ap(slot).rearrange("p (s n) -> p s n", n=64),
                    rt1.ap(0).rearrange("p (s n) -> p s n", n=64),
                    float(qk_scale), rbc, ALU.mult, ALU.mult),
                    reads=[rt1.r[0], l2t.r[0]], writes=[qkstg.r[slot]])

            def tr_to(dstbuf, dsttile, dstcol, srcap, srcres):
                pb, pr = ps.tr()
                prog.op("pe", lambda h: h.transpose(pb[:, 0:128], srcap,
                                                    IDB),
                        reads=[srcres, IDB_R], writes=[pr])
                prog.op("dve", lambda h: h.tensor_copy(
                    dstbuf.ap(dsttile)[:, dstcol:dstcol + 128],
                    pb[:, 0:128]),
                    reads=[pr], writes=[dstbuf.r[dsttile]])
            for j, nm in enumerate(["rotAq", "rotBq", "rotAk", "rotBk"]):
                prog.dma("sp", "rot", lambda h, j=j, nm=nm: h.dma_start(
                    rot.ap(j), dram["c_" + nm]),
                    writes=[rot.r[j]])
            fq = dram["fqs"].tensor
            fk = dram["fks"].tensor
            for j, (tn, so) in enumerate([(fq, 0), (fq, 32), (fk, 0),
                                          (fk, 32)]):
                if so == 0:
                    ap = bass.AP(tn, layer * 64, [[0, 128], [1, 64]])
                else:
                    ap = bass.AP(tn, layer * 64 + 32,
                                 [[0, 128], [-32, 2], [1, 32]])
                if so == 0:
                    dst = rep.ap(0)[:, 64 * j:64 * (j + 1)]
                else:
                    dst = rep.ap(0)[:, 64 * j:64 * (j + 1)].rearrange(
                        "p (a n) -> p a n", a=2)
                prog.dma("sp", "rep", lambda h, dst=dst, ap=ap: h.dma_start(
                    dst, ap), writes=[rep.r[0]])
            for j in range(4):
                rv = rep.ap(0)[:, 64 * j:64 * (j + 1)].unsqueeze(1)\
                    .broadcast_to([128, 16, 64])
                prog.op("dve", lambda h, j=j, rv=rv: h.tensor_mul(
                    rot.ap(j).rearrange("p (s n) -> p s n", n=64),
                    rot.ap(j).rearrange("p (s n) -> p s n", n=64), rv),
                    reads=[rot.r[j], rep.r[0]], writes=[rot.r[j]])
            for tt in range(TT):
                prog.op("dve", lambda h, tt=tt: h.memset(
                    v_ext.ap(tt).rearrange("p (s n) -> p s n",
                                           n=65)[:, :, 64:65], 1.0),
                    writes=[v_ext.r[tt]])
            wt = dram["fWqkv"].tensor
            woff = layer * C * 3 * C

            def ldqkv(nch, bi):
                ap = bass.AP(wt, woff + 512 * nch,
                             [[3 * C, 128], [128 * 3 * C, KT], [1, 512]])
                prog.dma("pool", "wq%d" % bi, lambda h, ap=ap, bi=bi:
                         h.dma_start(wq_ring[bi].ap(0).rearrange(
                             "p (k n) -> p k n", n=512), ap),
                         writes=[wq_ring[bi].r[0]])

            def qkv_mm(nch, bi, tt):
                pb, pr = ps.mm()

                def emit(h, nch=nch, bi=bi, tt=tt, pb=pb):
                    ins = None
                    w3 = wq_ring[bi].ap(0).rearrange("p (k n) -> p k n",
                                                     n=512)
                    for k in range(KT):
                        ins = h.matmul(pb[:, 0:512],
                                       shz[k][:, 128 * tt:128 * tt + 128],
                                       w3[:, k, :],
                                       start=(k == 0), stop=(k == KT - 1))
                    return ins
                prog.op("pe", emit,
                        reads=[wq_ring[bi].r[0]] + [zbf.r[k]
                                                    for k in range(KT)],
                        writes=[pr])
                return pb, pr

            for nch in range(2, 6):
                bi = nch % 2
                ldqkv(nch, bi)
                for tt in range(TT):
                    pb, pr = qkv_mm(nch, bi, tt)
                    if nch >= 4:
                        v3 = v_ext.ap(tt).rearrange("p (s n) -> p s n", n=65)
                        vsl = v3[:, 8 * (nch - 4):8 * (nch - 3), 0:64]
                        prog.op("act", lambda h, pb=pb, vsl=vsl: h.activation(
                            vsl, pb[:, 0:512].rearrange(
                                "p (s n) -> p s n", n=64), AF.Copy),
                            reads=[pr], writes=[v_ext.r[tt]])
                    else:
                        slot = tt % 2
                        rotary_unit(pb, pr, tt, 1.0, rot.ap(2), rot.r[2],
                                    rot.ap(3), rot.r[3], slot)
                        for dtl in range(4):
                            tr_to(kTf, 4 * (nch - 2) + dtl, 128 * tt,
                                  qkstg.ap(slot)[:, 128 * dtl:128 * dtl + 128],
                                  qkstg.r[slot])
            for cc in range(4):
                c0 = 512 * cc
                for nch in range(2):
                    bi = nch % 2
                    ldqkv(nch, bi)
                    for ttl in range(4):
                        tt = 4 * cc + ttl
                        pb, pr = qkv_mm(nch, bi, tt)
                        slot = ttl % 2
                        rotary_unit(pb, pr, tt, 8.0, rot.ap(0), rot.r[0],
                                    rot.ap(1), rot.r[1], slot)
                        for dtl in range(4):
                            tr_to(qT, 4 * nch + dtl, 128 * ttl,
                                  qkstg.ap(slot)[:, 128 * dtl:128 * dtl + 128],
                                  qkstg.r[slot])
                for itl in range(4):
                    git = 4 * cc + itl
                    ic = 128 * itl
                    ob, obr = outk.ap(itl % 2), outk.r[itl % 2]
                    jts = [j for j in (git - 2, git - 1, git) if j >= 0]
                    for g in range(8):
                        for sub, hh in ((0, 2 * g), (64, 2 * g + 1)):
                            pi = (2 * g + sub // 64) % 3
                            ptile, ptr = PT.ap(pi), PT.r[pi]
                            pbS, prS = ps.mm()

                            def emitS(h, jts=jts, hh=hh, sub=sub, pbS=pbS,
                                      ic=ic):
                                ins = None
                                for idx, jt in enumerate(jts):
                                    ins = h.matmul(
                                        pbS[:, 128 * idx:128 * (idx + 1)],
                                        kTf.ap(hh // 2)[sub:sub + 64,
                                                        128 * jt:128 * jt
                                                        + 128],
                                        qT.ap(hh // 2)[sub:sub + 64,
                                                       ic:ic + 128],
                                        start=True, stop=True,
                                        tile_position=(sub, 0))
                                return ins
                            prog.op("pe", emitS,
                                    reads=[kTf.r[hh // 2], qT.r[hh // 2]],
                                    writes=[prS])
                            nj = len(jts)
                            prog.op("act", lambda h, pbS=pbS, ptile=ptile,
                                    nj=nj: h.activation(
                                        ptile[:, 0:128 * nj],
                                        pbS[:, 0:128 * nj], AF.Exp),
                                    reads=[prS], writes=[ptr])
                            for idx, jt in enumerate(jts):
                                mk = None
                                if jt == git:
                                    mk, mkr = maskU, maskU_r
                                elif jt == git - 2:
                                    mk, mkr = maskL, maskL_r
                                if mk is not None:
                                    prog.op("dve", lambda h, ptile=ptile,
                                            idx=idx, mk=mk: h.tensor_mul(
                                                ptile[:, 128 * idx:128 *
                                                      (idx + 1)],
                                                ptile[:, 128 * idx:128 *
                                                      (idx + 1)], mk),
                                            reads=[ptr, mkr], writes=[ptr])
                            pbO, prO = ps.mm()

                            def emitPV(h, jts=jts, hh=hh, ptile=ptile,
                                       pbO=pbO):
                                ins = None
                                for idx, jt in enumerate(jts):
                                    ins = h.matmul(
                                        pbO[:, 0:65],
                                        ptile[:, 128 * idx:128 * (idx + 1)],
                                        v_ext.ap(jt)[:, 65 * hh:65 * hh + 65],
                                        start=(idx == 0),
                                        stop=(idx == len(jts) - 1))
                                return ins
                            prog.op("pe", emitPV,
                                    reads=[ptr] + [v_ext.r[jt] for jt in jts],
                                    writes=[prO])
                            prog.op("dve", lambda h, pbO=pbO: h.reciprocal(
                                den.ap(0)[:, 0:1], pbO[:, 64:65]),
                                reads=[prO], writes=[den.r[0]], hazard=True)
                            prog.op("dve", lambda h, pbO=pbO, hh=hh, ob=ob:
                                    h.tensor_mul(
                                        ob[:, 64 * hh:64 * hh + 64]
                                        .unsqueeze(1),
                                        pbO[:, 0:64].unsqueeze(1),
                                        den.ap(0)[:, 0:1].unsqueeze(2)
                                        .broadcast_to([128, 1, 64])),
                                    reads=[prO, den.r[0]], writes=[obr])
                    for dt_ in range(KT):
                        tr_to(aoT, dt_, 128 * itl,
                              ob[:, 128 * dt_:128 * dt_ + 128], obr)
                load_fm(zstg2, "z2ld", z_pkt, c0, 512, zd_cells(c0, 512))
                wot = dram["fWo"].tensor
                wooff = layer * C * C
                for nt in range(KT):
                    bi = nt % 2
                    ap = bass.AP(wot, wooff + 128 * nt,
                                 [[C, 128], [128 * C, KT], [1, 128]])
                    prog.dma("pool", "wo%d" % bi, lambda h, ap=ap, bi=bi:
                             h.dma_start(wo_ring[bi].ap(0).rearrange(
                                 "p (k n) -> p k n", n=128), ap),
                             writes=[wo_ring[bi].r[0]])
                    pb, pr = ps.mm()

                    def emit(h, nt=nt, pb=pb, bi=bi):
                        ins = None
                        w3 = wo_ring[bi].ap(0).rearrange("p (k n) -> p k n",
                                                         n=128)
                        for k in range(KT):
                            ins = h.matmul(pb[:, 0:512], w3[:, k, :],
                                           aoT.ap(k)[:, 0:512],
                                           start=(k == 0), stop=(k == KT - 1))
                        return ins
                    prog.op("pe", emit,
                            reads=[wo_ring[bi].r[0]] + [aoT.r[k]
                                                        for k in range(KT)],
                            writes=[pr])
                    prog.op("dve", lambda h, nt=nt, pb=pb:
                            h.scalar_tensor_tensor(
                                zstg2.ap(nt), pb[:, 0:512], 1.0,
                                zstg2.ap(nt), ALU.mult, ALU.add),
                            reads=[pr, zstg2.r[nt]], writes=[zstg2.r[nt]])
                prog.dma("sp", "zst", lambda h, c0=c0: h.dma_start(
                    z_pkt[:, :, c0:c0 + 512], _grp_ap(zstg2, 512)),
                    reads=list(zstg2.r), writes=zd_cells(c0, 512))

        def ffn_phase(layer):
            zreg.reset(prog)
            oreg.reset(prog)
            xn = zreg.carve(KT, 1024, BF16)
            gated = zreg.carve(22, 1024, BF16)
            w1_ring = [oreg.carve(1, KT * 256, BF16) for _ in range(2)]
            w2_ring = [oreg.carve(1, 22 * 128, BF16) for _ in range(2)]
            zstg = oreg.carve(KT, 516, F32)
            halbuf = oreg.carve(1, 8, F32)
            gelu_sb = oreg.carve(2, 512, F32)
            bcols = oreg.carve(1, 32, F32)
            b1cols = oreg.carve(1, 48, F32)
            ln_carves()
            rows, sqb, bcb = B["rows"], B["sqb"], B["bcb"]
            zstg2 = B["zstg2"]
            prog.dma("sp", "hal", lambda h: h.dma_start(
                halbuf.ap(0)[:, 0:8], z_pkt[:, :, 1023:1024],
                allow_slow_non_contiguous=True),
                reads=zd_cells(1023, 1), writes=list(halbuf.r))
            load_cols(bcols.ap(0)[:, 0:8], bcols.r[0], dram["flng"].tensor,
                      layer * C, 8)
            load_cols(bcols.ap(0)[:, 8:16], bcols.r[0], dram["flnb"].tensor,
                      layer * C, 8)
            load_cols(bcols.ap(0)[:, 16:24], bcols.r[0], dram["fb2"].tensor,
                      layer * C, 8)
            load_cols(b1cols.ap(0)[:, 0:21], b1cols.r[0], dram["fb1"].tensor,
                      layer * W1N, 21)
            load_cols(b1cols.ap(0)[:, 21:22], b1cols.r[0], dram["fb1"].tensor,
                      layer * W1N + 2688, 1, width=42)
            load_cols(b1cols.ap(0)[:, 24:45], b1cols.r[0], dram["fb1"].tensor,
                      layer * W1N + 2730, 21)
            load_cols(b1cols.ap(0)[:, 45:46], b1cols.r[0], dram["fb1"].tensor,
                      layer * W1N + 5418, 1, width=42)
            w1t = dram["fW1"].tensor
            w2t = dram["fW2"].tensor
            for c2 in range(2):
                C0 = 1024 * c2
                for sub in range(2):
                    c0 = C0 + 512 * sub
                    if c0 == 0:
                        for k in range(KT):
                            prog.op("dve", lambda h, k=k: h.memset(
                                zstg.ap(k)[:, 0:1], 0.0),
                                writes=[zstg.r[k]])
                        prog.dma("sp", "zld", lambda h: h.dma_start(
                            bass.AP(zstg.ap(0).tensor, zstg.ap(0).offset + 1,
                                    [[zstg.ap(0).ap[0][0], 128], [516, KT],
                                     [1, 512]]),
                            z_pkt[:, :, 0:512]),
                            reads=zd_cells(0, 512), writes=list(zstg.r))
                    else:
                        prog.dma("sp", "zld", lambda h, c0=c0: h.dma_start(
                            bass.AP(zstg.ap(0).tensor, zstg.ap(0).offset,
                                    [[zstg.ap(0).ap[0][0], 128], [516, KT],
                                     [1, 513]]),
                            z_pkt[:, :, c0 - 1:c0 + 512]),
                            reads=zd_cells(c0 - 1, 513), writes=list(zstg.r))
                        if c0 == 1024:
                            for k in range(KT):
                                prog.op("dve", lambda h, k=k: h.tensor_copy(
                                    zstg.ap(k)[:, 0:1],
                                    halbuf.ap(0)[:, k:k + 1]),
                                    reads=[halbuf.r[0]], writes=[zstg.r[k]])

                    def outw(k, mu_bc, rstd_bc, bc_r, sub=sub):
                        off = 1 if k < 4 else 0
                        prog.op("dve", lambda h, k=k, off=off:
                                h.scalar_tensor_tensor(
                                    zstg2.ap(k), zstg.ap(k)[:, off:off + 512],
                                    1.0, mu_bc, ALU.mult, ALU.subtract),
                                reads=[zstg.r[k], bc_r[0], bc_r[1]], writes=[zstg2.r[k]])
                        prog.op("dve", lambda h, k=k: h.scalar_tensor_tensor(
                            zstg2.ap(k), zstg2.ap(k),
                            bcols.ap(0)[:, k:k + 1], rstd_bc,
                            ALU.mult, ALU.mult),
                            reads=[zstg2.r[k], bc_r[0], bc_r[1], bcols.r[0]],
                            writes=[zstg2.r[k]])
                        prog.op("act", lambda h, k=k, sub=sub: h.activation(
                            xn.ap(k)[:, 512 * sub:512 * sub + 512],
                            zstg2.ap(k), AF.Identity,
                            bias=bcols.ap(0)[:, 8 + k:9 + k]),
                            reads=[zstg2.r[k], bcols.r[0]], writes=[xn.r[k]])

                    def fsrc(k):
                        off = 1 if k < 4 else 0
                        return (zstg.ap(k)[:, off:off + 512], zstg.r[k])

                    def bfsrc(k, c0=c0):
                        off = PAD if k < 4 else PAD - 1
                        return (zbf.ap(k)[:, off + c0:off + c0 + 512],
                                zbf.r[k])
                    ln_chunk(512, fsrc, bfsrc, outw, rows, sqb, bcb)
                if DBG_XN[0] and c2 == 1 and layer == 0:
                    dbg_pkt7 = dbg_t.rearrange("(k p) t -> p k t", p=128)
                    for k in range(KT):
                        prog.op("dve", lambda h, k=k: h.tensor_copy(
                            zstg2.ap(k), xn.ap(k)[:, 0:512]),
                            reads=[xn.r[k]], writes=[zstg2.r[k]])
                    prog.dma("sp", "dbgs", lambda h: h.dma_start(
                        dbg_pkt7[:, :, 0:512], _grp_ap(zstg2, 512)),
                        reads=list(zstg2.r), writes=[Res()])
                for p in range(22):
                    bi = p % 2
                    pw = 128 if p < 21 else 42
                    apa = bass.AP(w1t, layer * C * W1N + 128 * p,
                                  [[W1N, 128], [128 * W1N, KT], [1, pw]])
                    apg = bass.AP(w1t, layer * C * W1N + 2730 + 128 * p,
                                  [[W1N, 128], [128 * W1N, KT], [1, pw]])
                    prog.dma("pool", "w1%d" % bi, lambda h, apa=apa, bi=bi,
                             pw=pw: h.dma_start(
                                 bass.AP(w1_ring[bi].ap(0).tensor,
                                         w1_ring[bi].ap(0).offset,
                                         [[w1_ring[bi].ap(0).ap[0][0], 128],
                                          [256, KT], [1, pw]]), apa),
                             writes=[w1_ring[bi].r[0]])
                    prog.dma("pool", "w1%d" % bi, lambda h, apg=apg, bi=bi,
                             pw=pw: h.dma_start(
                                 bass.AP(w1_ring[bi].ap(0).tensor,
                                         w1_ring[bi].ap(0).offset + 128,
                                         [[w1_ring[bi].ap(0).ap[0][0], 128],
                                          [256, KT], [1, pw]]), apg),
                             writes=[w1_ring[bi].r[0]])
                    for nsub in range(2):
                        pbA, prA = ps.mm()
                        pbB, prB = ps.mm()

                        def emitA(h, p=p, bi=bi, nsub=nsub, pbA=pbA, pw=pw):
                            ins = None
                            for k in range(KT):
                                lh = bass.AP(
                                    w1_ring[bi].ap(0).tensor,
                                    w1_ring[bi].ap(0).offset + 256 * k,
                                    [[w1_ring[bi].ap(0).ap[0][0], 128],
                                     [1, pw]])
                                ins = h.matmul(
                                    pbA[0:pw, 0:512], lh,
                                    xn.ap(k)[:, 512 * nsub:512 * nsub + 512],
                                    start=(k == 0), stop=(k == KT - 1))
                            return ins
                        prog.op("pe", emitA,
                                reads=[w1_ring[bi].r[0]] + [xn.r[k] for k in
                                                            range(KT)],
                                writes=[prA])

                        def emitB(h, p=p, bi=bi, nsub=nsub, pbB=pbB, pw=pw):
                            ins = None
                            for k in range(KT):
                                lh = bass.AP(
                                    w1_ring[bi].ap(0).tensor,
                                    w1_ring[bi].ap(0).offset + 256 * k + 128,
                                    [[w1_ring[bi].ap(0).ap[0][0], 128],
                                     [1, pw]])
                                ins = h.matmul(
                                    pbB[0:pw, 0:512], lh,
                                    xn.ap(k)[:, 512 * nsub:512 * nsub + 512],
                                    start=(k == 0), stop=(k == KT - 1))
                            return ins
                        prog.op("pe", emitB,
                                reads=[w1_ring[bi].r[0]] + [xn.r[k] for k in
                                                            range(KT)],
                                writes=[prB])
                        gsl = gelu_sb.ap(nsub)
                        gsr = gelu_sb.r[nsub]
                        prog.op("act", lambda h, pbB=pbB, gsl=gsl, p=p, pw=pw:
                                h.activation(gsl[0:pw, :], pbB[0:pw, 0:512],
                                             AF.Gelu,
                                             bias=b1cols.ap(0)[0:pw,
                                                               24 + p:25 + p]),
                                reads=[prB, b1cols.r[0]], writes=[gsr])
                        prog.op("dve", lambda h, pbA=pbA, gsl=gsl, p=p,
                                nsub=nsub, pw=pw: h.scalar_tensor_tensor(
                                    gated.ap(p)[0:pw, 512 * nsub:512 * nsub
                                                + 512],
                                    pbA[0:pw, 0:512],
                                    b1cols.ap(0)[0:pw, p:p + 1],
                                    gsl[0:pw, :], ALU.add, ALU.mult),
                                reads=[prA, gsr, b1cols.r[0]],
                                writes=[gated.r[p]])
                for nsub in range(2):
                    c0 = C0 + 512 * nsub
                    load_fm(zstg2, "z2ld", z_pkt, c0, 512, zd_cells(c0, 512))
                    for nt in range(KT):
                        bi = nt % 2
                        ap1 = bass.AP(w2t, layer * DIM_INNER * C + 128 * nt,
                                      [[C, 128], [128 * C, 21], [1, 128]])
                        ap2 = bass.AP(w2t, layer * DIM_INNER * C
                                      + 2688 * C + 128 * nt,
                                      [[C, 42], [1, 128]])
                        prog.dma("pool", "w2%d" % bi, lambda h, ap1=ap1,
                                 bi=bi: h.dma_start(
                                     bass.AP(w2_ring[bi].ap(0).tensor,
                                             w2_ring[bi].ap(0).offset,
                                             [[w2_ring[bi].ap(0).ap[0][0],
                                               128], [128, 21], [1, 128]]),
                                     ap1),
                                 writes=[w2_ring[bi].r[0]])
                        prog.dma("pool", "w2%d" % bi, lambda h, ap2=ap2,
                                 bi=bi: h.dma_start(
                                     w2_ring[bi].ap(0)[0:42,
                                                       21 * 128:22 * 128],
                                     ap2),
                                 writes=[w2_ring[bi].r[0]])
                        pb, pr = ps.mm()

                        def emit(h, nt=nt, bi=bi, nsub=nsub, pb=pb):
                            ins = None
                            for p in range(22):
                                pw = 128 if p < 21 else 42
                                ins = h.matmul(
                                    pb[:, 0:512],
                                    w2_ring[bi].ap(0)[0:pw, 128 * p:128 * p
                                                      + 128],
                                    gated.ap(p)[0:pw, 512 * nsub:512 * nsub
                                                + 512],
                                    start=(p == 0), stop=(p == 21))
                            return ins
                        prog.op("pe", emit,
                                reads=[w2_ring[bi].r[0]] + [gated.r[p] for p
                                                            in range(22)],
                                writes=[pr])
                        prog.op("dve", lambda h, nt=nt, pb=pb:
                                h.scalar_tensor_tensor(
                                    zstg2.ap(nt), pb[:, 0:512],
                                    bcols.ap(0)[:, 16 + nt:17 + nt],
                                    zstg2.ap(nt), ALU.add, ALU.add),
                                reads=[pr, zstg2.r[nt], bcols.r[0]],
                                writes=[zstg2.r[nt]])
                    prog.dma("sp", "zst", lambda h, c0=c0: h.dma_start(
                        z_pkt[:, :, c0:c0 + 512], _grp_ap(zstg2, 512)),
                        reads=list(zstg2.r), writes=zd_cells(c0, 512))

        oreg.reset(prog)
        xbstg = oreg.carve(KT, 512, BF16)
        ln_carves()
        rows, sqb, bcb = B["rows"], B["sqb"], B["bcb"]
        zstg2 = B["zstg2"]
        for c in range(4):
            c0 = 512 * c
            cells = [r_xfm[k][c] for k in range(KT)]
            load_fm(zstg2, "xst", xfm_pkt, c0, 512, cells)
            load_fm(xbstg, "xbst", xfm_pkt, c0, 512, cells, q="pool")

            def outw0(k, mu_bc, rstd_bc, bc_r, c0=c0):
                prog.op("dve", lambda h, k=k: h.scalar_tensor_tensor(
                    zstg2.ap(k), zstg2.ap(k), 1.0, mu_bc,
                    ALU.mult, ALU.subtract),
                    reads=[zstg2.r[k], bc_r[0], bc_r[1]], writes=[zstg2.r[k]])
                prog.op("dve", lambda h, k=k: h.scalar_tensor_tensor(
                    zstg2.ap(k), zstg2.ap(k), 1.0, rstd_bc,
                    ALU.mult, ALU.mult),
                    reads=[zstg2.r[k], bc_r[0], bc_r[1]], writes=[zstg2.r[k]])
                prog.op("act", lambda h, k=k, c0=c0: h.activation(
                    zbf.ap(k)[:, PAD + c0:PAD + c0 + 512], zstg2.ap(k),
                    AF.Copy),
                    reads=[zstg2.r[k]], writes=[zbf.r[k]])
            ln_chunk(512, lambda k: (zstg2.ap(k), zstg2.r[k]),
                     lambda k: (xbstg.ap(k), xbstg.r[k]),
                     outw0, rows, sqb, bcb)
            if c == 1 and DBG_LN[0]:
                dbg_pkt9 = dbg_t.rearrange("(k p) t -> k p t", p=128)
                prog.dma("sp", "dbgs", lambda h: h.dma_start(
                    dbg_pkt9[0][:, 0:1024], bcb.ap(0)),
                    reads=[bcb.r[0]], writes=[Res()])
                prog.dma("sp", "dbgs", lambda h: h.dma_start(
                    dbg_pkt9[1][0:2, 0:1024], 
                    bass.AP(rows.ap(0).tensor, rows.ap(0).offset,
                            [[rows.ap(0).ap[0][0], 2], [1, 1024]])),
                    reads=[rows.r[0], rows.r[1]], writes=[Res()])
            prog.dma("sp", "zst", lambda h, c0=c0: h.dma_start(
                z_pkt[:, :, c0:c0 + 512], _grp_ap(zstg2, 512)),
                reads=list(zstg2.r), writes=zd_cells(c0, 512))

        for layer in range(FFN_NL):
            if "a" in FFN_PARTS:
                attn_phase(layer)
                zbf_refresh()
            if "f" in FFN_PARTS:
                ffn_phase(layer)
                zbf_refresh()

        if not FFN_EPI:
            return
        oreg.reset(prog)
        ep2 = oreg.carve(KT, 512, F32)
        ep1 = oreg.carve(KT, 516, F32)
        for c in range(4):
            c0 = 512 * c
            cells = [r_xfm[k][c] for k in range(KT)]
            load_fm(ep2, "xst", xfm_pkt, c0, 512, cells)
            load_fm(ep1, "zld", z_pkt, c0, 512, zd_cells(c0, 512))
            for k in range(KT):
                prog.op("dve", lambda h, k=k: h.tensor_add(
                    ep2.ap(k), ep2.ap(k), ep1.ap(k)[:, 0:512]),
                    reads=[ep2.r[k], ep1.r[k]], writes=[ep2.r[k]])
            store_fm(ep2, xfm_pkt, c0, 512, cells)

    def attn_b():
        r_zdb = [[Res() for _ in range(4)] for _ in range(KT)]

        def zdb_cells(c0, w):
            g0, g1 = c0 // 512, (c0 + w - 1) // 512
            return [r_zdb[k][g] for k in range(KT) for g in range(g0, g1 + 1)]
        ZD_CELLS[0] = zdb_cells
        zbf_grp2 = bass.AP(zbf.ap(0).tensor, zbf.ap(0).offset + PAD,
                           [[zbf.ap(0).ap[0][0], 128], [ZW, KT], [1, T]])

        def zbf_refresh2():
            prog.dma("pool", "zbfr", lambda h: h.dma_start(zbf_grp2, z_pkt),
                     reads=zdb_cells(0, T), writes=list(zbf.r))

        oreg.reset(prog)
        zreg.reset(prog)
        xbstg = oreg.carve(KT, 512, BF16)
        rows = oreg.carve(2, 1024, F32)
        sqb = oreg.carve(2, 512, BF16)
        bcb = oreg.carve(1, 1024, F32)
        pstg = oreg.carve(KT, 512, F32)
        colsc_stage[0] = oreg.carve(1, 128, F32)
        for c in range(4):
            c0 = 512 * c
            cells = [r_xfm[k][c] for k in range(KT)]
            load_fm(pstg, "xst", xfm_pkt, c0, 512, cells)
            load_fm(xbstg, "xbst", xfm_pkt, c0, 512, cells, q="pool")

            def outwb(k, mu_bc, rstd_bc, bc_r, c0=c0):
                prog.op("dve", lambda h, k=k: h.scalar_tensor_tensor(
                    pstg.ap(k), pstg.ap(k), 1.0, mu_bc,
                    ALU.mult, ALU.subtract),
                    reads=[pstg.r[k], bc_r[0], bc_r[1]], writes=[pstg.r[k]])
                prog.op("dve", lambda h, k=k: h.scalar_tensor_tensor(
                    pstg.ap(k), pstg.ap(k), 1.0, rstd_bc,
                    ALU.mult, ALU.mult),
                    reads=[pstg.r[k], bc_r[0], bc_r[1]], writes=[pstg.r[k]])
                prog.op("act", lambda h, k=k, c0=c0: h.activation(
                    zbf.ap(k)[:, PAD + c0:PAD + c0 + 512], pstg.ap(k),
                    AF.Copy),
                    reads=[pstg.r[k]], writes=[zbf.r[k]])
            ln_chunk(512, lambda k: (pstg.ap(k), pstg.r[k]),
                     lambda k: (xbstg.ap(k), xbstg.r[k]),
                     outwb, rows, sqb, bcb)
            prog.dma("sp", "zst", lambda h, c0=c0: h.dma_start(
                z_pkt[:, :, c0:c0 + 512], _grp_ap(pstg, 512)),
                reads=list(pstg.r), writes=zdb_cells(c0, 512))

        for j in range(SSM_NL):
            ssm_layer(j, zdb_cells)
            zbf_refresh2()
        if SSM_B5:
            dense_attn("b5")

    def ssm_layer(lay, zdb_cells):
        zreg.reset(prog)
        oreg.reset(prog)
        xs = zreg.carve(16, 2051, BF16)
        BmCm = oreg.carve(1, 2048, F32)
        u_slab = oreg.carve(16, 512, BF16)
        utok = oreg.carve(1, 2048, BF16)
        y_slab = u_slab
        Hs = oreg.carve(1, 2048, F32)
        Hs_bf = oreg.carve(1, 2048, BF16)
        cmsc = oreg.carve(1, 256, BF16)
        bmw = oreg.carve(1, 64, BF16)
        bmt = oreg.carve(1, 128, BF16)
        wi_ring = [oreg.carve(1, KT * 128, BF16) for _ in range(2)]
        wos_ring = [oreg.carve(1, 16 * 128, BF16) for _ in range(2)]
        cwcols = oreg.carve(1, 64, F32)
        cbcols = oreg.carve(1, 16, F32)
        lncols = oreg.carve(1, 16, F32)
        rows = Buf(oreg.carve(1, 1024, F32).aps + oreg.carve(1, 512, F32).aps,
                   [Res(), Res()])
        sqb = oreg.carve(2, 512, BF16)
        bcb = oreg.carve(1, 1024, F32)
        zstg2 = oreg.carve(KT, 512, F32)
        xbstg = oreg.carve(KT, 512, BF16)
        xc = oreg.carve(2, 512, F32)
        ptbuf = oreg.carve(1, 128, BF16)
        colsc_stage[0] = oreg.carve(1, 128, F32)
        decD, decD_r = cbuf["decayD"].ap(0), cbuf["decayD"].r[0]
        decC, decC_r = cbuf["decayCol"].ap(0), cbuf["decayCol"].r[0]
        wu, wu_r = cbuf["decayWu"].ap(0), cbuf["decayWu"].r[0]

        load_cols(cbcols.ap(0)[:, 0:16], cbcols.r[0], dram["bconvb"].tensor,
                  lay * D_INNER, 16)
        load_cols(lncols.ap(0)[:, 0:8], lncols.r[0], dram["bssmg"].tensor,
                  lay * C, 8)
        load_cols(lncols.ap(0)[:, 8:16], lncols.r[0], dram["bssmb"].tensor,
                  lay * C, 8)
        cwsrc = bass.AP(dram["bconvw"].tensor, lay * D_INNER * 4,
                        [[4, 128], [512, 16], [1, 4]])
        prog.dma("sp", "cw", lambda h: h.dma_start(
            cwcols.ap(0).rearrange("p (s n) -> p s n", n=4), cwsrc),
            writes=[cwcols.r[0]])
        for kt in range(16):
            prog.op("dve", lambda h, kt=kt: h.memset(xs.ap(kt)[:, 0:3], 0.0),
                    writes=[xs.r[kt]])
        prog.op("dve", lambda h: h.memset(Hs.ap(0)[0:64, :], 0.0),
                writes=[Hs.r[0]])
        prog.op("dve", lambda h: h.memset(Hs_bf.ap(0)[0:64, :], 0.0),
                writes=[Hs_bf.r[0]])

        wit = dram["bWi"].tensor
        wioff = lay * C * 2176
        for cc in range(4):
            c0 = 512 * cc
            for nt in range(17):
                bi = nt % 2
                ap = bass.AP(wit, wioff + 128 * nt,
                             [[2176, 128], [128 * 2176, KT], [1, 128]])
                prog.dma("pool", "wi%d" % bi, lambda h, ap=ap, bi=bi:
                         h.dma_start(wi_ring[bi].ap(0).rearrange(
                             "p (k n) -> p k n", n=128), ap),
                         writes=[wi_ring[bi].r[0]])
                pb, pr = ps.mm()

                def emit(h, nt=nt, bi=bi, c0=c0, pb=pb):
                    ins = None
                    w3 = wi_ring[bi].ap(0).rearrange("p (k n) -> p k n",
                                                     n=128)
                    for k in range(KT):
                        ins = h.matmul(pb[:, 0:512], w3[:, k, :],
                                       zbf.ap(k)[:, PAD + c0:PAD + c0 + 512],
                                       start=(k == 0), stop=(k == KT - 1))
                    return ins
                prog.op("pe", emit,
                        reads=[wi_ring[bi].r[0]] + [zbf.r[k]
                                                    for k in range(KT)],
                        writes=[pr])
                if nt < 16:
                    prog.op("act", lambda h, nt=nt, c0=c0, pb=pb:
                            h.activation(xs.ap(nt)[:, 3 + c0:3 + c0 + 512],
                                         pb[:, 0:512], AF.Copy),
                            reads=[pr], writes=[xs.r[nt]])
                else:
                    prog.op("dve", lambda h, c0=c0, pb=pb: h.tensor_copy(
                        BmCm.ap(0)[:, c0:c0 + 512], pb[:, 0:512]),
                        reads=[pr], writes=[BmCm.r[0]])

        wost = dram["bWo"].tensor
        wosoff = lay * D_INNER * C
        for sl in range(4):
            s0_ = 512 * sl
            for kt in range(16):
                xcs, xcr = xc.ap(sl % 2), xc.r[sl % 2]
                prog.op("dve", lambda h, kt=kt, s0_=s0_, xcs=xcs:
                        h.tensor_scalar_mul(
                            xcs, xs.ap(kt)[:, s0_ + 3:s0_ + 3 + 512],
                            cwcols.ap(0)[:, 4 * kt + 3:4 * kt + 4]),
                        reads=[xs.r[kt], cwcols.r[0]], writes=[xcr])
                for jj in range(3):
                    prog.op("dve", lambda h, kt=kt, s0_=s0_, jj=jj, xcs=xcs:
                            h.scalar_tensor_tensor(
                                xcs, xs.ap(kt)[:, s0_ + jj:s0_ + jj + 512],
                                cwcols.ap(0)[:, 4 * kt + jj:4 * kt + jj + 1],
                                xcs, ALU.mult, ALU.add),
                            reads=[xs.r[kt], cwcols.r[0], xcr], writes=[xcr])
                prog.op("act", lambda h, kt=kt, xcs=xcs: h.activation(
                    u_slab.ap(kt), xcs, AF.Silu,
                    bias=cbcols.ap(0)[:, kt:kt + 1]),
                    reads=[xcr, cbcols.r[0]], writes=[u_slab.r[kt]])
            for ch in range(4):
                c0 = s0_ + 128 * ch
                lc = 128 * ch
                for kt in range(16):
                    pb, pr = ps.tr()
                    prog.op("pe", lambda h, kt=kt, lc=lc, pb=pb: h.transpose(
                        pb[:, 0:128], u_slab.ap(kt)[:, lc:lc + 128], IDB),
                        reads=[u_slab.r[kt], IDB_R], writes=[pr])
                    prog.op("dve", lambda h, kt=kt, pb=pb: h.tensor_copy(
                        utok.ap(0)[:, 128 * kt:128 * kt + 128], pb[:, 0:128]),
                        reads=[pr], writes=[utok.r[0]])
                prog.op("dve", lambda h, c0=c0: h.tensor_mul(
                    cmsc.ap(0)[0:64, 0:128],
                    BmCm.ap(0)[64:128, c0:c0 + 128], decC[64:128, :]),
                    reads=[BmCm.r[0], decC_r], writes=[cmsc.r[0]])
                prog.op("dve", lambda h, c0=c0: h.tensor_copy(
                    bmt.ap(0)[0:64, :], BmCm.ap(0)[0:64, c0:c0 + 128]),
                    reads=[BmCm.r[0]], writes=[bmt.r[0]])
                pbt, prt = ps.tr()
                prog.op("pe", lambda h, pbt=pbt: h.transpose(
                    pbt[:, 0:64], bmt.ap(0)[0:64, 0:128], IDB[0:64, 0:64]),
                    reads=[bmt.r[0], IDB_R], writes=[prt])
                prog.op("dve", lambda h, pbt=pbt: h.tensor_scalar_mul(
                    bmw.ap(0)[:, 0:64], pbt[:, 0:64], wu[:, 0:1]),
                    reads=[prt, wu_r], writes=[bmw.r[0]])
                prog.op("dve", lambda h, c0=c0: h.tensor_copy(
                    cmsc.ap(0)[0:64, 128:256],
                    BmCm.ap(0)[64:128, c0:c0 + 128]),
                    reads=[BmCm.r[0]], writes=[cmsc.r[0]])
                pbG, prG = ps.mm()
                prog.op("pe", lambda h, pbG=pbG: h.matmul(
                    pbG[:, 0:128], bmt.ap(0)[0:64, 0:128],
                    cmsc.ap(0)[0:64, 128:256], start=True, stop=True),
                    reads=[bmt.r[0], cmsc.r[0]], writes=[prG])
                prog.op("dve", lambda h, pbG=pbG: h.tensor_mul(
                    ptbuf.ap(0)[:, 0:128], pbG[:, 0:128], decD),
                    reads=[prG, decD_r], writes=[ptbuf.r[0]])
                for kt in range(16):
                    pby, pry = ps.mm()
                    prog.op("pe", lambda h, kt=kt, pby=pby: h.matmul(
                        pby[:, 0:128], Hs_bf.ap(0)[0:64, 128 * kt:128 * kt
                                                   + 128],
                        cmsc.ap(0)[0:64, 0:128], start=True, stop=False),
                        reads=[Hs_bf.r[0], cmsc.r[0]], writes=[pry])
                    prog.op("pe", lambda h, kt=kt, pby=pby: h.matmul(
                        pby[:, 0:128],
                        utok.ap(0)[:, 128 * kt:128 * kt + 128],
                        ptbuf.ap(0)[:, 0:128], start=False, stop=True),
                        reads=[utok.r[0], ptbuf.r[0]], writes=[pry])
                    prog.op("dve", lambda h, kt=kt, lc=lc, pby=pby:
                            h.tensor_copy(
                                y_slab.ap(kt)[:, lc:lc + 128], pby[:, 0:128]),
                            reads=[pry], writes=[y_slab.r[kt]])
                for dc_ in range(4):
                    pbh, prh = ps.mm()
                    prog.op("pe", lambda h, dc_=dc_, pbh=pbh: h.matmul(
                        pbh[0:64, 0:512], bmw.ap(0)[:, 0:64],
                        utok.ap(0)[:, 512 * dc_:512 * dc_ + 512],
                        start=True, stop=True),
                        reads=[bmw.r[0], utok.r[0]], writes=[prh])
                    prog.op("dve", lambda h, dc_=dc_, pbh=pbh:
                            h.scalar_tensor_tensor(
                                Hs.ap(0)[0:64, 512 * dc_:512 * dc_ + 512],
                                Hs.ap(0)[0:64, 512 * dc_:512 * dc_ + 512],
                                float(0.5 ** 128), pbh[0:64, 0:512],
                                ALU.mult, ALU.add),
                            reads=[Hs.r[0], prh], writes=[Hs.r[0]])
                prog.op("dve", lambda h: h.tensor_copy(
                    Hs_bf.ap(0)[0:64, :], Hs.ap(0)[0:64, :]),
                    reads=[Hs.r[0]], writes=[Hs_bf.r[0]])
            load_fm(zstg2, "z2ld", z_pkt, s0_, 512, zdb_cells(s0_, 512))
            for nt in range(KT):
                bi = nt % 2
                ap = bass.AP(wost, wosoff + 128 * nt,
                             [[C, 128], [128 * C, 16], [1, 128]])
                prog.dma("pool", "ws%d" % bi, lambda h, ap=ap, bi=bi:
                         h.dma_start(wos_ring[bi].ap(0).rearrange(
                             "p (k n) -> p k n", n=128), ap),
                         writes=[wos_ring[bi].r[0]])
                pb, pr = ps.mm()

                def emit(h, nt=nt, bi=bi, pb=pb):
                    ins = None
                    w3 = wos_ring[bi].ap(0).rearrange("p (k n) -> p k n",
                                                      n=128)
                    for kt in range(16):
                        ins = h.matmul(pb[:, 0:512], w3[:, kt, :],
                                       y_slab.ap(kt)[:, 0:512],
                                       start=(kt == 0), stop=(kt == 15))
                    return ins
                prog.op("pe", emit,
                        reads=[wos_ring[bi].r[0]] + [y_slab.r[kt]
                                                     for kt in range(16)],
                        writes=[pr])
                prog.op("dve", lambda h, nt=nt, pb=pb: h.scalar_tensor_tensor(
                    zstg2.ap(nt), pb[:, 0:512], 1.0, zstg2.ap(nt),
                    ALU.mult, ALU.add),
                    reads=[pr, zstg2.r[nt]], writes=[zstg2.r[nt]])
            for k in range(KT):
                prog.op("act", lambda h, k=k: h.activation(
                    xbstg.ap(k), zstg2.ap(k), AF.Copy),
                    reads=[zstg2.r[k]], writes=[xbstg.r[k]])

            def outws(k, mu_bc, rstd_bc, bc_r):
                prog.op("dve", lambda h, k=k: h.scalar_tensor_tensor(
                    zstg2.ap(k), zstg2.ap(k), 1.0, mu_bc,
                    ALU.mult, ALU.subtract),
                    reads=[zstg2.r[k], bc_r[0], bc_r[1]], writes=[zstg2.r[k]])
                prog.op("dve", lambda h, k=k: h.scalar_tensor_tensor(
                    zstg2.ap(k), zstg2.ap(k), lncols.ap(0)[:, k:k + 1],
                    rstd_bc, ALU.mult, ALU.mult),
                    reads=[zstg2.r[k], bc_r[0], bc_r[1], lncols.r[0]],
                    writes=[zstg2.r[k]])
                prog.op("act", lambda h, k=k: h.activation(
                    zstg2.ap(k), zstg2.ap(k), AF.Identity,
                    bias=lncols.ap(0)[:, 8 + k:9 + k]),
                    reads=[zstg2.r[k], lncols.r[0]], writes=[zstg2.r[k]],
                    hazard=True)
            ln_chunk(512, lambda k: (zstg2.ap(k), zstg2.r[k]),
                     lambda k: (xbstg.ap(k), xbstg.r[k]),
                     outws, rows, sqb, bcb)
            prog.dma("sp", "zst", lambda h, s0_=s0_: h.dma_start(
                z_pkt[:, :, s0_:s0_ + 512], _grp_ap(zstg2, 512)),
                reads=list(zstg2.r), writes=zdb_cells(s0_, 512))

    load_consts()
    if "s0" in stages:
        s0()
    attn_bufs = None
    if "dbgpt" in stages:
        pass
    if "attn_a" in stages:
        attn_bufs = dense_attn("a")
    if "ffn1" in stages:
        ffnblk()
    if "attn_b" in stages:
        attn_b()
    if "b5only" in stages:
        dense_attn("b5")
    if "ffn2" in stages:
        ffnblk()
    if "dbg_kt" in stages:
        dstg2 = attn_bufs["xstg"]
        kTb = attn_bufs["kT"]
        dbg_pkt3 = dbg_t.rearrange("(k p) t -> p k t", p=128)
        for grp in range(4):
            for k in range(KT):
                prog.op("dve", lambda h, k=k, grp=grp: h.tensor_copy(
                    dstg2.ap(k), kTb.ap(k)[:, 512 * grp:512 * (grp + 1)]),
                    reads=[kTb.r[k]], writes=[dstg2.r[k]])
            prog.dma("sp", "dbgs", lambda h, grp=grp: h.dma_start(
                dbg_pkt3[:, :, 512 * grp:512 * (grp + 1)], _grp_ap(dstg2, 512)),
                reads=list(dstg2.r), writes=[Res()])
    if "dbg_outk" in stages:
        ob = attn_bufs["outk"]
        dstg3f = attn_bufs["xstg"]
        dbg_pkt5 = dbg_t.rearrange("(k p) t -> p k t", p=128)
        for k in range(KT):
            prog.op("dve", lambda h, k=k: h.tensor_copy(
                dstg3f.ap(k)[:, 0:128], ob.ap(1)[:, 128 * k:128 * (k + 1)]),
                reads=[ob.r[1]], writes=[dstg3f.r[k]])
        prog.dma("sp", "dbgs", lambda h: h.dma_start(
            dbg_pkt5[:, :, 0:512], _grp_ap(dstg3f, 512)),
            reads=list(dstg3f.r), writes=[Res()])
    if "dbg_aot" in stages:
        dstg3 = attn_bufs["aoT"]
        dstg3f = attn_bufs["xstg"]
        dbg_pkt4 = dbg_t.rearrange("(k p) t -> p k t", p=128)
        for k in range(KT):
            prog.op("dve", lambda h, k=k: h.tensor_copy(
                dstg3f.ap(k), dstg3.ap(k)),
                reads=[dstg3.r[k]], writes=[dstg3f.r[k]])
        prog.dma("sp", "dbgs", lambda h: h.dma_start(
            dbg_pkt4[:, :, 0:512], _grp_ap(dstg3f, 512)),
            reads=list(dstg3f.r), writes=[Res()])
    if "dbg_z" in stages:
        oreg.reset(prog)
        dstg = oreg.carve(KT, 512, F32)
        dbg_pkt = dbg_t.rearrange("(k p) t -> p k t", p=128)
        for grp in range(4):
            prog.dma("sp", "dbgx", lambda h, grp=grp: h.dma_start(
                _grp_ap(dstg, 512), z_pkt[:, :, 512 * grp:512 * (grp + 1)]),
                reads=prog.frontier_res(), writes=list(dstg.r))
            prog.dma("sp", "dbgs", lambda h, grp=grp: h.dma_start(
                dbg_pkt[:, :, 512 * grp:512 * (grp + 1)], _grp_ap(dstg, 512)),
                reads=list(dstg.r), writes=[Res()])
    if "dbg_zbf" in stages:
        oreg.reset(prog)
        dstg = oreg.carve(KT, 512, F32)
        dbg_pkt2 = dbg_t.rearrange("(k p) t -> p k t", p=128)
        for grp in range(4):
            for k in range(KT):
                prog.op("dve", lambda h, k=k, grp=grp: h.tensor_copy(
                    dstg.ap(k), zbf.ap(k)[:, PAD + 512 * grp:PAD + 512 * (grp + 1)]),
                    reads=[zbf.r[k]], writes=[dstg.r[k]])
            prog.dma("sp", "dbgs", lambda h, grp=grp: h.dma_start(
                dbg_pkt2[:, :, 512 * grp:512 * (grp + 1)], _grp_ap(dstg, 512)),
                reads=list(dstg.r), writes=[Res()])
    if "dbg_xfm" in stages:
        oreg.reset(prog)
        dstg = oreg.carve(KT, 512, F32)
        dbg_pkt = dbg_t.rearrange("(k p) t -> p k t", p=128)
        for grp in range(4):
            load_fm(dstg, "dbgx", xfm_pkt, 512 * grp, 512,
                    [r_xfm[k][grp] for k in range(KT)])
            prog.dma("sp", "dbgs", lambda h, grp=grp: h.dma_start(
                dbg_pkt[:, :, 512 * grp:512 * (grp + 1)], _grp_ap(dstg, 512)),
                reads=list(dstg.r), writes=[Res()])
    if "s5" in stages:
        s5()
    return nc, prog


def kernel(**inputs):
    nc, prog = build_program(STAGES)
    with nc.Block() as block:
        prog.finalize(block)

    CN = host_consts()
    xfull = np.asarray(inputs["L0_attn_a_x"], dtype=np.float32)
    base = {}
    ren = {
        "aWq": "L0_attn_a_Wq", "aWk": "L0_attn_a_Wk", "aWv": "L0_attn_a_Wv",
        "aWo": "L0_attn_a_Wo",
        "bWi": "L0_attn_b_in_proj_w", "bconvw": "L0_attn_b_conv_w",
        "bWo": "L0_attn_b_out_proj_w",
        "bwq": "L0_attn_b_wq", "bwk": "L0_attn_b_wk", "bwv": "L0_attn_b_wv",
        "bwo": "L0_attn_b_wo",
        "fWqkv": "L0_ffn_Wqkv", "fWo": "L0_ffn_Wo", "fW1": "L0_ffn_W1",
        "fW2": "L0_ffn_W2",
    }
    for k, v in ren.items():
        base[k] = np.ascontiguousarray(np.asarray(inputs[v], dtype=np.float32))
    for k, v in {"abq": "L0_attn_a_bq", "abk": "L0_attn_a_bk",
                 "abv": "L0_attn_a_bv", "abo": "L0_attn_a_bo",
                 "batng": "L0_attn_b_attn_ln_g",
                 "batnb": "L0_attn_b_attn_ln_b"}.items():
        base[k] = np.ascontiguousarray(
            np.asarray(inputs[v], dtype=np.float32).reshape(1, -1))
    for k, v in {"bconvb": "L0_attn_b_conv_b", "bssmg": "L0_attn_b_ssm_ln_g",
                 "bssmb": "L0_attn_b_ssm_ln_b", "fqs": "L0_ffn_q_scale",
                 "fks": "L0_ffn_k_scale", "flng": "L0_ffn_ln_g",
                 "flnb": "L0_ffn_ln_b", "fb1": "L0_ffn_b1",
                 "fb2": "L0_ffn_b2"}.items():
        a = np.asarray(inputs[v], dtype=np.float32)
        base[k] = np.ascontiguousarray(a.reshape(a.shape[0], 1, -1))
    for nm, arr in CN.items():
        base["c_" + nm] = np.ascontiguousarray(arr)

    in_maps = []
    for b in range(B):
        m = dict(base)
        m["x"] = np.ascontiguousarray(xfull[b])
        in_maps.append(m)
    res = run_bass_kernel_spmd(nc, in_maps, core_ids=list(range(B)))
    return np.stack([np.asarray(r["out"], dtype=np.float32)
                     for r in res.results], axis=0)
